# version 24
# baseline (speedup 1.0000x reference)
import numpy as np
import concourse.bass as bass
import concourse.mybir as mybir
from concourse.bass_utils import run_bass_kernel_spmd

F32 = mybir.dt.float32
BF16 = mybir.dt.bfloat16
AF = mybir.ActivationFunctionType
ALU = mybir.AluOpType

NCORES = 8
T = 2048
CT = 8
D = 1024
CH = 512
CHB = 256
EPS_LN = 1e-5
EPS_Z = 1e-6
EPS_WKV = 1e-9
EPS_GN = 1e-5

ENGINES = ("tensor", "vector", "scalar", "gpsimd", "sync")
PARTS = ("attn_a", "ffn1", "attn_b", "ffn2")
DEBUG = False
DISABLE_KNOW = False
TAP_SEQ = 0


class Prog:
    def __init__(self):
        self.ops = []
        self.segs = {}

    def _ranges(self, buf):
        return self.segs.setdefault(buf, [[0, 1 << 60, -1, set()]])

    def _read(self, buf, lo, hi, opid):
        deps = set()
        for seg in self._ranges(buf):
            if seg[0] < hi and lo < seg[1]:
                if seg[2] >= 0:
                    deps.add(seg[2])
                seg[3].add(opid)
        return deps

    def _write(self, buf, lo, hi, opid):
        deps = set()
        segs = self._ranges(buf)
        newsegs = []
        for seg in segs:
            if seg[0] < hi and lo < seg[1]:
                if seg[2] >= 0:
                    deps.add(seg[2])
                deps |= seg[3]
                if seg[0] < lo:
                    newsegs.append([seg[0], lo, seg[2], set(seg[3])])
                if hi < seg[1]:
                    newsegs.append([hi, seg[1], seg[2], set(seg[3])])
            else:
                newsegs.append(seg)
        newsegs.append([lo, hi, opid, set()])
        newsegs.sort(key=lambda s: s[0])
        self.segs[buf] = newsegs
        return deps

    def add(self, eng, fn, reads=(), writes=(), dma=False):
        opid = len(self.ops)
        deps = set()
        for (b, lo, hi) in reads:
            deps |= self._read(b, lo, hi, opid)
        for (b, lo, hi) in writes:
            deps |= self._write(b, lo, hi, opid)
        deps.discard(opid)
        self.ops.append(dict(eng=eng, fn=fn, dma=dma, deps=deps))
        return opid

    def finalize(self, sems):
        n = len(self.ops)
        dom = [("d", op["eng"]) if op["dma"] else ("c", op["eng"])
               for op in self.ops]
        needed = [set() for _ in range(n)]
        tagged = [False] * n
        for j, op_ in enumerate(self.ops):
            if op_["dma"]:
                tagged[j] = True
            for i in op_["deps"]:
                if dom[i] == ("c", op_["eng"]):
                    continue
                needed[j].add(i)
                tagged[i] = True
        cnt = {}
        val = [0] * n
        for i in range(n):
            if tagged[i]:
                c = cnt.get(dom[i], 0) + 1
                cnt[dom[i]] = c
                val[i] = c
        clocks = {e: {} for e in ENGINES}
        knowledge = {}
        emit = {e: [] for e in ENGINES}
        for j, op_ in enumerate(self.ops):
            E = op_["eng"]
            C = clocks[E]
            for i in sorted(needed[j]):
                d = dom[i]
                v = val[i] * (16 if self.ops[i]["dma"] else 1)
                if C.get(d, 0) >= v:
                    continue
                sem = sems[d]
                emit[E].append(lambda eng, s=sem, w=v: eng.wait_ge(s, w))
                C[d] = v
                know = knowledge.get(i)
                if know and not DISABLE_KNOW:
                    for kd, kv in know.items():
                        if C.get(kd, 0) < kv:
                            C[kd] = kv
            if tagged[j]:
                d = dom[j]
                v = val[j] * (16 if op_["dma"] else 1)
                fn = op_["fn"]
                sem = sems[d]
                inc = 16 if op_["dma"] else 1
                emit[E].append(lambda eng, f=fn, s=sem, ic=inc: f(eng).then_inc(s, ic))
                kn = dict(C)
                kn[d] = v
                knowledge[j] = kn
            else:
                emit[E].append(op_["fn"])
        return emit


def _bs(col_lo, col_hi, itemsize):
    return col_lo * itemsize, col_hi * itemsize


def build_program(parts=PARTS):
    from contextlib import ExitStack
    nc = bass.Bass("TRN2", target_bir_lowering=False)
    P = Prog()
    stack = ExitStack()
    E = stack.enter_context

    x_in = nc.dram_tensor("x", [2, D, T], F32, kind="ExternalInput").ap()
    wall = nc.dram_tensor("wall", [12, D, D], F32, kind="ExternalInput").ap()
    vpp_in = nc.dram_tensor("vpp", [128, 72], F32, kind="ExternalInput").ap()
    vrow_in = nc.dram_tensor("vrow", [1, 5120], F32, kind="ExternalInput").ap()
    ident_in = nc.dram_tensor("ident", [128, 128], F32, kind="ExternalInput").ap()
    y_out = nc.dram_tensor("y", [2, D, T], F32, kind="ExternalOutput").ap()
    wbf = nc.dram_tensor("wbf", [12, 128, CT * 1024], BF16, kind="Internal").ap()
    slab_hbm = nc.dram_tensor("slabh", [128, CT * T], BF16, kind="Internal").ap()

    def sb(name, shape, dt):
        return E(nc.sbuf_tensor(name, shape, dt))

    R = sb("R", [128, CT * T], F32)
    XN = sb("XN", [128, 2 * CT * (CH + 1)], BF16)
    WS = [sb(f"ws{i}", [128, CT * 1024], BF16) for i in range(3)]
    VPP = sb("vpp_s", [128, 72], F32)
    VPPC = sb("vppc", [128, 16], F32)
    AB = sb("ab", [128, 16], F32)
    VROWB = sb("vrowb", [1, 5120], BF16)
    ONEROW = sb("onerow", [1, 512], BF16)
    SUMN = sb("sumn", [128, 1], F32)
    SUMB = sb("sumb", [128, 1], BF16)
    SUMNB = sb("sumnb", [128, 1], BF16)
    ONECOL = sb("onecol", [128, 1], BF16)
    XB = sb("xb", [128, 2 * 512], BF16)
    EPSC = sb("epsc", [128, 1], F32)
    EPSW = sb("epsw", [128, 1], F32)
    STATW = sb("statw", [1, 1024], F32)
    STATB0 = sb("statb0", [1, 2048], BF16)
    GNS = sb("gns", [128, 64], F32)
    GNQ = sb("gnq", [128, 64], F32)
    BLK = sb("blk", [128, 128], BF16)
    KVS = sb("kvs", [128, CT * 128], BF16)
    KS = sb("ks", [128, CT * 16], BF16)
    KSCOL = sb("kscol", [128, 8], F32)
    HSEL = sb("hsel", [128, CT * 128], BF16)
    ZPAD = sb("zpad", [128, 512], BF16)
    XSQ = sb("xsq", [128, 2 * 512], BF16)
    DTMP = sb("dtmp", [128, 2 * 512], F32)
    ARENA = sb("arena", [128, 11664], F32)

    PB = [E(nc.psum_tensor(f"pb{i}", [128, 512], F32)) for i in range(8)]

    sems = {}
    for e in ENGINES:
        sems[("c", e)] = E(nc.semaphore(f"c_{e}"))
    for e in ("sync", "gpsimd", "scalar"):
        sems[("d", e)] = E(nc.semaphore(f"d_{e}"))

    def op(eng, fn, reads=(), writes=(), dma=False):
        return P.add(eng, fn, reads, writes, dma)

    tap_names = []
    cur_seq = [0]

    def tap(name, ap, spans, dt):
        if not DEBUG or name in tap_names or cur_seq[0] != TAP_SEQ:
            return
        cols = ap.shape[1]
        t = nc.dram_tensor(f"dbg_{name}", [ap.shape[0], cols], dt, kind="ExternalOutput").ap()
        tap_names.append(name)
        op("gpsimd", lambda e, o=t, i_=ap: e.dma_start(out=o, in_=i_),
           reads=spans, writes=[(f"dbg_{name}", 0, 1)], dma=True)
        op("gpsimd", lambda e: e.nop(), reads=[(f"dbg_{name}", 0, 1)])

    def mm(bank_ap, lhsT_ap, rhs_ap, reads, writes, start, stop):
        op("tensor",
           lambda e, o=bank_ap, l=lhsT_ap, r=rhs_ap, st=start, sp=stop:
               e.matmul(out=o, lhsT=l, rhs=r, start=st, stop=sp),
           reads=reads, writes=writes)

    def PBr(i):
        return [(f"pb{i}", 0, 4096)]

    ar = [0]

    def arena(cols_f32):
        off = ar[0]
        ar[0] += cols_f32
        assert ar[0] <= 11664, f"arena overflow {ar[0]}"
        return off

    def arena_reset():
        ar[0] = 0

    def A_f32(off, w):
        return ARENA[:, off:off + w], [("arena",) + _bs(off, off + w, 4)]

    def A_bf(off_f32, w_bf):
        ap = ARENA[:, off_f32:off_f32 + (w_bf + 1) // 2].bitcast(BF16)
        return ap, [("arena", off_f32 * 4, off_f32 * 4 + w_bf * 2)]

    WSTG_OFF = 11664 - 4096
    PREPB_OFF = WSTG_OFF - 2048
    WSTG = ARENA[:, WSTG_OFF:WSTG_OFF + 4096]
    PREPB = ARENA[:, PREPB_OFF:PREPB_OFF + 2048].bitcast(BF16)

    def WSTG_sp(lo, hi):
        return [("arena",) + _bs(WSTG_OFF + lo, WSTG_OFF + hi, 4)]

    def PREPB_sp(lo, hi):
        return [("arena", PREPB_OFF * 4 + lo * 2, PREPB_OFF * 4 + hi * 2)]

    op("gpsimd", lambda e: e.dma_start(out=VPP[:], in_=vpp_in),
       writes=[("vpp", 0, 288)], dma=True)
    op("vector", lambda e: e.memset(ONEROW[:], 1.0), writes=[("onerow", 0, 1024)])
    op("vector", lambda e: e.memset(SUMN[:], -1.0 / D), writes=[("sumn", 0, 4)])
    op("vector", lambda e: e.memset(SUMB[:], 1.0 / D), writes=[("sumb", 0, 2)])
    op("vector", lambda e: e.memset(SUMNB[:], -1.0 / D), writes=[("sumnb", 0, 2)])
    op("vector", lambda e: e.memset(ONECOL[:], 1.0), writes=[("onecol", 0, 2)])
    op("vector", lambda e: e.memset(EPSC[:], EPS_LN), writes=[("epsc", 0, 4)])
    op("vector", lambda e: e.memset(EPSW[:], EPS_WKV), writes=[("epsw", 0, 4)])
    for pi, (plo, phi) in enumerate(((0, 2048), (2048, 4096), (4096, 5120))):
        w = phi - plo
        op("sync", lambda e, lo=plo, w_=w: e.dma_start(out=WSTG[0:1, 0:w_],
                                                       in_=vrow_in[0:1, lo:lo + w_]),
           writes=WSTG_sp(0, 2048), dma=True)
        op("scalar", lambda e, lo=plo, w_=w: e.copy(out=VROWB[0:1, lo:lo + w_],
                                                    in_=WSTG[0:1, 0:w_]),
           reads=WSTG_sp(0, 2048), writes=[("vrowb",) + _bs(plo, phi, 2)])
    op("scalar", lambda e: e.activation(out=VPPC[:, 0:8], in_=VPP[:, 48:56], func=AF.Exp),
       reads=[("vpp", 192, 224)], writes=[("vppc", 0, 32)])
    op("scalar", lambda e: e.activation(out=VPPC[:, 8:16], in_=VPP[:, 40:48], func=AF.Exp),
       reads=[("vpp", 160, 192)], writes=[("vppc", 32, 64)])
    op("scalar", lambda e: e.activation(out=VPPC[:, 8:16], in_=VPPC[:, 8:16],
                                        func=AF.Exp, scale=-1.0),
       reads=[("vppc", 32, 64)], writes=[("vppc", 32, 64)])
    op("vector", lambda e: e.memset(BLK[:], 0.0), writes=[("blk", 0, 256)])
    op("vector", lambda e: e.memset(BLK[0:64, 0:64], 1.0), writes=[("blk", 0, 256)])
    op("vector", lambda e: e.memset(BLK[64:128, 64:128], 1.0), writes=[("blk", 0, 256)])
    op("vector", lambda e: e.memset(ZPAD[:], 0.0), writes=[("zpad", 0, 1024)])
    op("vector", lambda e: e.memset(HSEL[:], 0.0), writes=[("hsel", 0, CT * 256)])
    for p_ in range(CT):
        op("gpsimd", lambda e, p=p_:
           e.dma_start(out=HSEL[2 * p:2 * p + 1, p * 128: p * 128 + 64],
                       in_=ONEROW[0:1, 0:64]),
           reads=[("onerow", 0, 1024)],
           writes=[("hsel",) + _bs(p_ * 128, p_ * 128 + 64, 2)], dma=True)
        op("gpsimd", lambda e, p=p_:
           e.dma_start(out=HSEL[2 * p + 1:2 * p + 2, p * 128 + 64: p * 128 + 128],
                       in_=ONEROW[0:1, 0:64]),
           reads=[("onerow", 0, 1024)],
           writes=[("hsel",) + _bs(p_ * 128 + 64, p_ * 128 + 128, 2)], dma=True)
    for ct in range(CT):
        op("gpsimd", lambda e, ct_=ct:
           e.dma_start(out=R[:, ct_ * T:(ct_ + 1) * T],
                       in_=x_in[0, ct_ * 128:(ct_ + 1) * 128, :]),
           writes=[("R",) + _bs(ct * T, (ct + 1) * T, 4)], dma=True)

    for i in range(12):
        for pc in range(4):
            buf = pc % 2
            stg = WSTG[:, buf * 2048:(buf + 1) * 2048]
            stg_sp = WSTG_sp(buf * 2048, (buf + 1) * 2048)
            pbuf = PREPB[:, buf * 2048:(buf + 1) * 2048]
            pbuf_sp = PREPB_sp(buf * 2048, (buf + 1) * 2048)
            for a_ in range(2):
                op("sync", lambda e, s=stg, i_=i, pc_=pc, aa_=a_:
                   e.dma_start(out=s[:, aa_ * 1024:(aa_ + 1) * 1024],
                               in_=wall[i_, pc_ * 256 + aa_ * 128: pc_ * 256 + (aa_ + 1) * 128, :]),
                   writes=stg_sp, dma=True)
            op("gpsimd", lambda e, s=stg, p=pbuf: e.tensor_copy(out=p, in_=s),
               reads=stg_sp, writes=pbuf_sp)
            op("gpsimd", lambda e, p=pbuf, i_=i, pc_=pc:
               e.dma_start(out=wbf[i_, :, pc_ * 2048:(pc_ + 1) * 2048], in_=p),
               reads=pbuf_sp,
               writes=[(f"wbf{i}",) + _bs(pc * 2048, (pc + 1) * 2048, 2)], dma=True)

    def load_weight(widx, slot):
        op("scalar", lambda e, s=WS[slot], i_=widx: e.dma_start(out=s[:], in_=wbf[i_]),
           reads=[(f"wbf{widx}", 0, CT * 1024 * 2)],
           writes=[(f"ws{slot}", 0, CT * 1024 * 2)], dma=True)

    def wsl(slot, ct, col_lo, col_hi):
        ap = WS[slot][:, ct * 1024 + col_lo: ct * 1024 + col_hi]
        sp = [(f"ws{slot}",) + _bs(ct * 1024 + col_lo, ct * 1024 + col_hi, 2)]
        return ap, sp

    LNF = [(2, 3, 4), (5, 6, 7)]
    LNP = [(4, 5), (6, 7)]

    def ln_stats(t0, w, par, b_st, b_sq):
        Rsp = lambda ct: [("R",) + _bs(ct * T + t0, ct * T + t0 + w, 4)]
        Rap = lambda ct: R[:, ct * T + t0: ct * T + t0 + w]
        for ct in range(CT):
            buf = ct % 2
            xb = XB[:, buf * 512: buf * 512 + w]
            xb_sp = [("xb",) + _bs(buf * 512, buf * 512 + w, 2)]
            xs = XSQ[:, buf * 512: buf * 512 + w]
            xs_sp = [("xsq",) + _bs(buf * 512, buf * 512 + w, 2)]
            op("scalar", lambda e, o=xb, i_=Rap(ct): e.copy(out=o, in_=i_),
               reads=Rsp(ct), writes=xb_sp)
            mm(PB[b_st][0:1, 0:w], SUMNB[:], xb,
               reads=[("sumnb", 0, 2)] + xb_sp, writes=PBr(b_st),
               start=(ct == 0), stop=(ct == CT - 1))
            op("scalar", lambda e, o=xs, i_=xb: e.activation(out=o, in_=i_, func=AF.Square),
               reads=xb_sp, writes=xs_sp)
            mm(PB[b_sq][0:1, 0:w], SUMB[:], xs,
               reads=[("sumb", 0, 2)] + xs_sp, writes=PBr(b_sq),
               start=(ct == 0), stop=(ct == CT - 1))
        st_sp = [("statw", 0, 4096)]
        sb0 = par * 1024
        stb_sp = [("statb0",) + _bs(sb0, sb0 + 1024, 2)]
        op("vector", lambda e: e.tensor_copy(out=STATW[0:1, 512:512 + w], in_=PB[b_st][0:1, 0:w]),
           reads=PBr(b_st), writes=st_sp)
        op("vector", lambda e: e.tensor_tensor(out=STATW[0:1, 0:w], in0=STATW[0:1, 512:512 + w],
                                               in1=STATW[0:1, 512:512 + w], op=ALU.mult),
           reads=st_sp, writes=st_sp)
        op("vector", lambda e: e.tensor_sub(out=STATW[0:1, 0:w], in0=PB[b_sq][0:1, 0:w],
                                            in1=STATW[0:1, 0:w]),
           reads=PBr(b_sq) + st_sp, writes=st_sp)
        op("scalar", lambda e: e.activation(out=STATW[0:1, 0:w], in_=STATW[0:1, 0:w],
                                            func=AF.Sqrt, bias=EPSC[0:1]),
           reads=st_sp + [("epsc", 0, 4)], writes=st_sp)
        op("vector", lambda e: e.reciprocal(out=STATW[0:1, 0:w], in_=STATW[0:1, 0:w]),
           reads=st_sp, writes=st_sp)
        op("vector", lambda e, o=STATB0[0:1, sb0 + 512:sb0 + 512 + w]:
           e.tensor_tensor(out=o, in0=STATW[0:1, 512:512 + w],
                           in1=STATW[0:1, 0:w], op=ALU.mult),
           reads=st_sp, writes=stb_sp)
        op("vector", lambda e, o=STATB0[0:1, sb0:sb0 + w]:
           e.tensor_copy(out=o, in_=STATW[0:1, 0:w]),
           reads=st_sp, writes=stb_sp)

    def ln_apply_full(t0, w, par, guard, b_nmr, b_rstd):
        sb0 = par * 1024
        stb_sp = [("statb0",) + _bs(sb0, sb0 + 1024, 2)]
        Rsp = lambda ct: [("R",) + _bs(ct * T + t0, ct * T + t0 + w, 4)]
        Rap = lambda ct: R[:, ct * T + t0: ct * T + t0 + w]
        mm(PB[b_nmr][:, 0:w], ONEROW[:, 0:128], STATB0[0:1, sb0 + 512:sb0 + 512 + w],
           reads=[("onerow", 0, 256)] + stb_sp, writes=PBr(b_nmr), start=True, stop=True)
        mm(PB[b_rstd][:, 0:w], ONEROW[:, 0:128], STATB0[0:1, sb0:sb0 + w],
           reads=[("onerow", 0, 256)] + stb_sp, writes=PBr(b_rstd), start=True, stop=True)
        for ct in range(CT):
            xn0 = par * CT * (CH + 1) + ct * (CH + 1)
            if guard == "zero":
                op("gpsimd", lambda e, o=XN[:, xn0:xn0 + 1]: e.memset(o, 0.0),
                   writes=[("xn",) + _bs(xn0, xn0 + 1, 2)])
            elif guard == "carry":
                oth = (1 - par) * CT * (CH + 1) + ct * (CH + 1)
                op("gpsimd", lambda e, o=XN[:, xn0:xn0 + 1], i_=XN[:, oth + CH:oth + CH + 1]:
                   e.tensor_copy(out=o, in_=i_),
                   reads=[("xn",) + _bs(oth + CH, oth + CH + 1, 2)],
                   writes=[("xn",) + _bs(xn0, xn0 + 1, 2)])
            buf = ct % 2
            dt_ = DTMP[:, buf * 512: buf * 512 + w]
            dt_sp = [("dtmp",) + _bs(buf * 512, buf * 512 + w, 4)]
            op("vector", lambda e, o=dt_, a=Rap(ct): e.tensor_add(out=o, in0=a, in1=PB[b_nmr][:, 0:w]),
               reads=Rsp(ct) + PBr(b_nmr), writes=dt_sp)
            xn_ap_ = XN[:, xn0 + 1: xn0 + 1 + w]
            xn_sp = [("xn",) + _bs(xn0 + 1, xn0 + 1 + w, 2)]
            op("vector", lambda e, o=xn_ap_, a=dt_:
               e.tensor_tensor(out=o, in0=a, in1=PB[b_rstd][:, 0:w], op=ALU.mult),
               reads=dt_sp + PBr(b_rstd), writes=xn_sp)

    def ln_apply_packed(t0, w, par, b_bc):
        sb0 = par * 1024
        stb_sp = [("statb0",) + _bs(sb0, sb0 + 1024, 2)]
        hw = w // 2
        for half in range(2):
            o0 = half * hw
            mm(PB[b_bc][:, 0:hw], ONEROW[:, 0:128],
               STATB0[0:1, sb0 + 512 + o0:sb0 + 512 + o0 + hw],
               reads=[("onerow", 0, 256)] + stb_sp, writes=PBr(b_bc), start=True, stop=True)
            mm(PB[b_bc][:, hw:2 * hw], ONEROW[:, 0:128],
               STATB0[0:1, sb0 + o0:sb0 + o0 + hw],
               reads=[("onerow", 0, 256)] + stb_sp, writes=PBr(b_bc), start=True, stop=True)
            for ct in range(CT):
                xn0 = par * CT * (CH + 1) + ct * (CH + 1)
                buf = ct % 2
                dt_ = DTMP[:, buf * 512: buf * 512 + hw]
                dt_sp = [("dtmp",) + _bs(buf * 512, buf * 512 + hw, 4)]
                rsp = [("R",) + _bs(ct * T + t0 + o0, ct * T + t0 + o0 + hw, 4)]
                op("vector", lambda e, o=dt_, a=R[:, ct * T + t0 + o0: ct * T + t0 + o0 + hw]:
                   e.tensor_add(out=o, in0=a, in1=PB[b_bc][:, 0:hw]),
                   reads=rsp + PBr(b_bc), writes=dt_sp)
                xsp = [("xn",) + _bs(xn0 + 1 + o0, xn0 + 1 + o0 + hw, 2)]
                op("vector", lambda e, o=XN[:, xn0 + 1 + o0: xn0 + 1 + o0 + hw], a=dt_:
                   e.tensor_tensor(out=o, in0=a, in1=PB[b_bc][:, hw:2 * hw], op=ALU.mult),
                   reads=dt_sp + PBr(b_bc), writes=xsp)

    def xn_ap(par, ct, lo, w):
        xn0 = par * CT * (CH + 1) + ct * (CH + 1)
        ap = XN[:, xn0 + 1 + lo: xn0 + 1 + lo + w]
        sp = [("xn",) + _bs(xn0 + 1 + lo, xn0 + 1 + lo + w, 2)]
        return ap, sp

    def r_update(ct, t0, w, bank):
        ap = R[:, ct * T + t0: ct * T + t0 + w]
        sp = [("R",) + _bs(ct * T + t0, ct * T + t0 + w, 4)]
        op("vector", lambda e, o=ap, b=PB[bank][:, 0:w]: e.tensor_add(out=o, in0=o, in1=b),
           reads=sp + PBr(bank), writes=sp)

    def ln_chunk0(t0, w, guard, b_st, b_nmr, b_rstd):
        Rsp = lambda ct: [("R",) + _bs(ct * T + t0, ct * T + t0 + w, 4)]
        Rap = lambda ct: R[:, ct * T + t0: ct * T + t0 + w]
        for ct in range(CT):
            mm(PB[b_st][0:1, 0:w], SUMN[:], Rap(ct),
               reads=[("sumn", 0, 4)] + Rsp(ct), writes=PBr(b_st),
               start=(ct == 0), stop=(ct == CT - 1))
        for ct in range(CT):
            buf = ct % 2
            xs = XSQ[:, buf * 512: buf * 512 + w]
            xs_sp = [("xsq",) + _bs(buf * 512, buf * 512 + w, 2)]
            op("scalar", lambda e, o=xs, i_=Rap(ct): e.activation(out=o, in_=i_, func=AF.Square),
               reads=Rsp(ct), writes=xs_sp)
            mm(PB[b_nmr][0:1, 0:w], SUMB[:], xs,
               reads=[("sumb", 0, 2)] + xs_sp, writes=PBr(b_nmr),
               start=(ct == 0), stop=(ct == CT - 1))
        st_sp = [("statw", 0, 4096)]
        stb_sp = [("statb0", 0, 2048)]
        op("vector", lambda e: e.tensor_copy(out=STATW[0:1, 512:512 + w], in_=PB[b_st][0:1, 0:w]),
           reads=PBr(b_st), writes=st_sp)
        op("vector", lambda e: e.tensor_tensor(out=STATW[0:1, 0:w], in0=STATW[0:1, 512:512 + w],
                                               in1=STATW[0:1, 512:512 + w], op=ALU.mult),
           reads=st_sp, writes=st_sp)
        op("vector", lambda e: e.tensor_sub(out=STATW[0:1, 0:w], in0=PB[b_nmr][0:1, 0:w],
                                            in1=STATW[0:1, 0:w]),
           reads=PBr(b_nmr) + st_sp, writes=st_sp)
        op("scalar", lambda e: e.activation(out=STATW[0:1, 0:w], in_=STATW[0:1, 0:w],
                                            func=AF.Sqrt, bias=EPSC[0:1]),
           reads=st_sp + [("epsc", 0, 4)], writes=st_sp)
        op("vector", lambda e: e.reciprocal(out=STATW[0:1, 0:w], in_=STATW[0:1, 0:w]),
           reads=st_sp, writes=st_sp)
        op("vector", lambda e: e.tensor_tensor(out=STATB0[0:1, 512:512 + w],
                                               in0=STATW[0:1, 512:512 + w],
                                               in1=STATW[0:1, 0:w], op=ALU.mult),
           reads=st_sp, writes=stb_sp)
        op("vector", lambda e: e.tensor_copy(out=STATB0[0:1, 0:w], in_=STATW[0:1, 0:w]),
           reads=st_sp, writes=stb_sp)
        mm(PB[b_nmr][:, 0:w], ONEROW[:, 0:128], STATB0[0:1, 512:512 + w],
           reads=[("onerow", 0, 256)] + stb_sp, writes=PBr(b_nmr), start=True, stop=True)
        mm(PB[b_rstd][:, 0:w], ONEROW[:, 0:128], STATB0[0:1, 0:w],
           reads=[("onerow", 0, 256)] + stb_sp, writes=PBr(b_rstd), start=True, stop=True)
        for ct in range(CT):
            xn0 = ct * (CH + 1)
            if guard == "zero":
                op("gpsimd", lambda e, o=XN[:, xn0:xn0 + 1]: e.memset(o, 0.0),
                   writes=[("xn",) + _bs(xn0, xn0 + 1, 2)])
            elif guard == "carry":
                op("gpsimd", lambda e, o=XN[:, xn0:xn0 + 1], i_=XN[:, xn0 + CH:xn0 + CH + 1]:
                   e.tensor_copy(out=o, in_=i_),
                   reads=[("xn",) + _bs(xn0 + CH, xn0 + CH + 1, 2)],
                   writes=[("xn",) + _bs(xn0, xn0 + 1, 2)])
            buf = ct % 2
            dt_ = DTMP[:, buf * 512: buf * 512 + w]
            dt_sp = [("dtmp",) + _bs(buf * 512, buf * 512 + w, 4)]
            op("vector", lambda e, o=dt_, a=Rap(ct): e.tensor_add(out=o, in0=a, in1=PB[b_nmr][:, 0:w]),
               reads=Rsp(ct) + PBr(b_nmr), writes=dt_sp)
            xn_ap_ = XN[:, xn0 + 1: xn0 + 1 + w]
            xn_sp = [("xn",) + _bs(xn0 + 1, xn0 + 1 + w, 2)]
            op("vector", lambda e, o=xn_ap_, a=dt_:
               e.tensor_tensor(out=o, in0=a, in1=PB[b_rstd][:, 0:w], op=ALU.mult),
               reads=dt_sp + PBr(b_rstd), writes=xn_sp)

    def xn_ap0(ct, lo, w):
        xn0 = ct * (CH + 1)
        ap = XN[:, xn0 + 1 + lo: xn0 + 1 + lo + w]
        sp = [("xn",) + _bs(xn0 + 1 + lo, xn0 + 1 + lo + w, 2)]
        return ap, sp

    KVBANK = [2, 2, 2, 3, 3, 3, 5, 5]
    KVCOL = [0, 129, 258, 0, 129, 258, 0, 129]

    def attn_a(sl, midload=None):
        wk_s, wv_s, wq_s, wg_s, wo_s = sl
        arena_reset()
        FMK = arena(512)
        VT = arena(520)
        QTMP = arena(512)
        FMQ = arena(2048)
        GATE = arena(2048)
        ZBG = arena(2048)
        PROD = arena(2048)
        vt, vt_sp = A_bf(VT, CT * 129)
        op("vector", lambda e: e.memset(vt, 1.0), writes=vt_sp)
        for h in range(T // CH):
            t0 = h * CH
            ln_chunk0(t0, CH, None, 6, 7, 6)
            for tt in range(CH // 128):
                tl = tt * 128
                first = (h == 0 and tt == 0)
                last = (h == T // CH - 1 and tt == CH // 128 - 1)
                POOL = [0, 1, 4]
                for half in range(2):
                    bank = POOL[(4 * tt + half) % 3]
                    for ct in range(CT):
                        lhs, lhs_sp = xn_ap0(ct, tl, 128)
                        rhs, rhs_sp = wsl(wk_s, ct, half * 512, half * 512 + 512)
                        mm(PB[bank][:, 0:512], lhs, rhs,
                           reads=lhs_sp + rhs_sp, writes=PBr(bank),
                           start=(ct == 0), stop=False)
                    boff = 1024 + half * 512
                    mm(PB[bank][:, 0:512], ONEROW[:, 0:128], VROWB[0:1, boff:boff + 512],
                       reads=[("onerow", 0, 256), ("vrowb",) + _bs(boff, boff + 512, 2)],
                       writes=PBr(bank), start=False, stop=True)
                fmk, fmk_sp = A_bf(FMK, CT * 128)
                for half in range(2):
                    bank = POOL[(4 * tt + half) % 3]
                    tmp = DTMP[:, 0:512]
                    tmp_sp = [("dtmp", 0, 2048)]
                    op("vector", lambda e, o=tmp, b=PB[bank][:, 0:512]:
                       e.tensor_single_scalar(out=o, in_=b, scalar=0.0, op=ALU.min),
                       reads=PBr(bank), writes=tmp_sp)
                    op("scalar", lambda e, o=tmp: e.activation(out=o, in_=o, func=AF.Exp),
                       reads=tmp_sp, writes=tmp_sp)
                    op("vector", lambda e, o=fmk[:, half * 512:(half + 1) * 512],
                       b=PB[bank][:, 0:512], t=tmp:
                       e.scalar_tensor_tensor(out=o, in0=b, scalar=0.0, in1=t,
                                              op0=ALU.max, op1=ALU.add),
                       reads=PBr(bank) + tmp_sp, writes=fmk_sp)
                for half in range(2):
                    bank = POOL[(4 * tt + half + 2) % 3]
                    for ct in range(CT):
                        lhs, lhs_sp = xn_ap0(ct, tl, 128)
                        rhs, rhs_sp = wsl(wv_s, ct, half * 512, half * 512 + 512)
                        mm(PB[bank][:, 0:512], lhs, rhs,
                           reads=lhs_sp + rhs_sp, writes=PBr(bank),
                           start=(ct == 0), stop=False)
                    boff = 2048 + half * 512
                    mm(PB[bank][:, 0:512], ONEROW[:, 0:128], VROWB[0:1, boff:boff + 512],
                       reads=[("onerow", 0, 256), ("vrowb",) + _bs(boff, boff + 512, 2)],
                       writes=PBr(bank), start=False, stop=True)
                    vdst = vt[:, half * 4 * 129:(half * 4 + 4) * 129]
                    op("scalar", lambda e, o=vdst, b=PB[bank][:, 0:512]:
                       e.copy(out=o.rearrange("p (a b) -> p a b", b=129)[:, :, 0:128],
                              in_=b.rearrange("p (a b) -> p a b", b=128)),
                       reads=PBr(bank), writes=vt_sp)
                for p in range(CT):
                    kb, kc = KVBANK[p], KVCOL[p]
                    mm(PB[kb][:, kc:kc + 129],
                       fmk[:, p * 128:(p + 1) * 128], vt[:, p * 129:(p + 1) * 129],
                       reads=fmk_sp + vt_sp, writes=PBr(kb),
                       start=(first and kc == 0), stop=(last and p in (2, 5, 7)))
        tap("fmk", A_bf(FMK, CT * 128)[0], A_bf(FMK, CT * 128)[1], BF16)
        tap("vt", A_bf(VT, CT * 129)[0], A_bf(VT, CT * 129)[1], BF16)
        op("vector", lambda e: e.memset(KVS[:], 0.0), writes=[("kvs", 0, CT * 256)])
        op("vector", lambda e: e.memset(KS[:], 0.0), writes=[("ks", 0, CT * 32)])
        for p in range(CT):
            kb, kc = KVBANK[p], KVCOL[p]
            op("vector", lambda e, p_=p, kb_=kb, kc_=kc:
               e.tensor_copy(out=KVS[0:64, p_ * 128: p_ * 128 + 64],
                             in_=PB[kb_][0:64, kc_:kc_ + 64]),
               reads=PBr(kb), writes=[("kvs",) + _bs(p * 128, p * 128 + 64, 2)])
            op("vector", lambda e, p_=p, kb_=kb, kc_=kc:
               e.tensor_copy(out=KVS[64:128, p_ * 128 + 64: p_ * 128 + 128],
                             in_=PB[kb_][64:128, kc_ + 64:kc_ + 128]),
               reads=PBr(kb), writes=[("kvs",) + _bs(p * 128 + 64, p * 128 + 128, 2)])
            op("vector", lambda e, p_=p, kb_=kb, kc_=kc:
               e.tensor_copy(out=KS[0:64, p_ * 16 + 2 * p_: p_ * 16 + 2 * p_ + 1],
                             in_=PB[kb_][0:64, kc_ + 128:kc_ + 129]),
               reads=PBr(kb), writes=[("ks",) + _bs(p * 16, (p + 1) * 16, 2)])
            op("vector", lambda e, p_=p, kb_=kb, kc_=kc:
               e.tensor_copy(out=KS[64:128, p_ * 16 + 2 * p_ + 1: p_ * 16 + 2 * p_ + 2],
                             in_=PB[kb_][64:128, kc_ + 128:kc_ + 129]),
               reads=PBr(kb), writes=[("ks",) + _bs(p * 16, (p + 1) * 16, 2)])
        tap("kvs", KVS[:], [("kvs", 0, CT * 256)], BF16)
        tap("ks", KS[:], [("ks", 0, CT * 32)], BF16)
        if midload:
            midload()
        for h in range(T // CH):
            t0 = h * CH
            ln_chunk0(t0, CH, None, 6, 4, 5)
            fmq, fmq_sp = A_bf(FMQ, CT * 512)
            for dt in range(CT):
                bank = dt % 2
                for ct in range(CT):
                    lhs, lhs_sp = wsl(wq_s, ct, dt * 128, (dt + 1) * 128)
                    rhs, rhs_sp = xn_ap0(ct, 0, CH)
                    mm(PB[bank][:, 0:CH], lhs, rhs,
                       reads=lhs_sp + rhs_sp, writes=PBr(bank),
                       start=(ct == 0), stop=False)
                boff = dt * 128
                mm(PB[bank][:, 0:CH], VROWB[0:1, boff:boff + 128], ONEROW[:, 0:CH],
                   reads=[("vrowb",) + _bs(boff, boff + 128, 2), ("onerow", 0, 1024)],
                   writes=PBr(bank), start=False, stop=True)
                qt, qt_sp = A_f32(QTMP, 512)
                op("vector", lambda e, o=qt, b=PB[bank][:, 0:CH]:
                   e.tensor_single_scalar(out=o, in_=b, scalar=0.0, op=ALU.min),
                   reads=PBr(bank), writes=qt_sp)
                op("scalar", lambda e, o=qt: e.activation(out=o, in_=o, func=AF.Exp),
                   reads=qt_sp, writes=qt_sp)
                op("vector", lambda e, o=fmq[:, dt * 512: dt * 512 + CH],
                   b=PB[bank][:, 0:CH], t=qt:
                   e.scalar_tensor_tensor(out=o, in0=b, scalar=0.0, in1=t,
                                          op0=ALU.max, op1=ALU.add),
                   reads=PBr(bank) + qt_sp, writes=fmq_sp)
            gate, gate_sp = A_bf(GATE, CT * 512)
            for dt in range(CT):
                bank = 2 + dt % 2
                for ct in range(CT):
                    lhs, lhs_sp = wsl(wg_s, ct, dt * 128, (dt + 1) * 128)
                    rhs, rhs_sp = xn_ap0(ct, 0, CH)
                    mm(PB[bank][:, 0:CH], lhs, rhs,
                       reads=lhs_sp + rhs_sp, writes=PBr(bank),
                       start=(ct == 0), stop=False)
                boff = 3072 + dt * 128
                mm(PB[bank][:, 0:CH], VROWB[0:1, boff:boff + 128], ONEROW[:, 0:CH],
                   reads=[("vrowb",) + _bs(boff, boff + 128, 2), ("onerow", 0, 1024)],
                   writes=PBr(bank), start=False, stop=True)
                op("scalar", lambda e, o=gate[:, dt * 512: dt * 512 + CH],
                   b=PB[bank][:, 0:CH]:
                   e.activation(out=o, in_=b, func=AF.Sigmoid),
                   reads=PBr(bank), writes=gate_sp)
            for p in range(CT):
                mm(PB[6][0:16, 0:CH], KS[:, p * 16:(p + 1) * 16],
                   fmq[:, p * 512: p * 512 + CH],
                   reads=[("ks",) + _bs(p * 16, (p + 1) * 16, 2)] + fmq_sp,
                   writes=PBr(6), start=(p == 0), stop=(p == CT - 1))
            qt, qt_sp = A_f32(QTMP, 512)
            op("vector", lambda e, o=qt[0:16, 0:CH]: e.tensor_single_scalar(
                out=o, in_=PB[6][0:16, 0:CH], scalar=EPS_Z, op=ALU.add),
               reads=PBr(6), writes=qt_sp)
            if h == 0:
                tap("den0", qt[0:16, 0:CH], qt_sp, F32)
            op("vector", lambda e, o=qt[0:16, 0:CH]: e.reciprocal(out=o, in_=o),
               reads=qt_sp, writes=qt_sp)
            op("vector", lambda e, o=ZPAD[0:16, 0:CH], i_=qt[0:16, 0:CH]:
               e.tensor_copy(out=o, in_=i_),
               reads=qt_sp, writes=[("zpad", 0, 1024)])
            if h == 0:
                tap("zr0", ZPAD[:], [("zpad", 0, 1024)], BF16)
            zbg, zbg_sp = A_bf(ZBG, CT * 512)
            for p in range(CT):
                mm(PB[4][:, 0:CH], HSEL[:, p * 128:(p + 1) * 128], ZPAD[:, 0:CH],
                   reads=[("hsel",) + _bs(p * 128, (p + 1) * 128, 2), ("zpad", 0, 1024)],
                   writes=PBr(4), start=True, stop=True)
                op("vector", lambda e, o=zbg[:, p * 512: p * 512 + CH],
                   g=gate[:, p * 512: p * 512 + CH]:
                   e.tensor_tensor(out=o, in0=PB[4][:, 0:CH], in1=g, op=ALU.mult),
                   reads=PBr(4) + gate_sp, writes=zbg_sp)
            prod, prod_sp = A_bf(PROD, CT * 512)
            for p in range(CT):
                bank = p % 4
                mm(PB[bank][:, 0:CH], KVS[:, p * 128:(p + 1) * 128],
                   fmq[:, p * 512: p * 512 + CH],
                   reads=[("kvs",) + _bs(p * 128, (p + 1) * 128, 2)] + fmq_sp,
                   writes=PBr(bank), start=True, stop=True)
                op("vector", lambda e, o=prod[:, p * 512: p * 512 + CH],
                   b=PB[bank][:, 0:CH], z=zbg[:, p * 512: p * 512 + CH]:
                   e.tensor_tensor(out=o, in0=b, in1=z, op=ALU.mult),
                   reads=PBr(bank) + zbg_sp, writes=prod_sp)
            if h == 0:
                tap("fmq0", A_bf(FMQ, CT * 512)[0], A_bf(FMQ, CT * 512)[1], BF16)
                tap("gate0", gate, gate_sp, BF16)
                tap("prod0", prod, prod_sp, BF16)
            for dt in range(CT):
                bank = 4 + dt % 2
                for ct in range(CT):
                    lhs, lhs_sp = wsl(wo_s, ct, dt * 128, (dt + 1) * 128)
                    mm(PB[bank][:, 0:CH], lhs, prod[:, ct * 512: ct * 512 + CH],
                       reads=lhs_sp + prod_sp, writes=PBr(bank),
                       start=(ct == 0), stop=False)
                boff = 4096 + dt * 128
                mm(PB[bank][:, 0:CH], VROWB[0:1, boff:boff + 128], ONEROW[:, 0:CH],
                   reads=[("vrowb",) + _bs(boff, boff + 128, 2), ("onerow", 0, 1024)],
                   writes=PBr(bank), start=False, stop=True)
                r_update(dt, t0, CH, bank)

    def ffn(sl):
        wr_s, wk_s, wv_s = sl
        arena_reset()
        DD = arena(256)
        XRK = arena(2048)
        RF = arena(2048)
        KACT = arena(2048)
        KT = arena(512)
        TMP = arena(1024)
        nch = T // CH

        def ln_f(h):
            par = h % 2
            ln_stats(h * CH, CH, par, LNF[par][0], LNF[par][1])
            ln_apply_full(h * CH, CH, par, "zero" if h == 0 else "carry",
                          LNF[par][1], LNF[par][2])

        ln_f(0)
        for h in range(nch):
            par = h % 2
            if h + 1 < nch:
                ln_f(h + 1)
            t0 = h * CH
            xr, xr_sp = A_bf(XRK, CT * 512)

            def mixes(vcol_base, dst, dst_sp):
                for ct in range(CT):
                    cur, cur_sp = xn_ap(par, ct, 0, CH)
                    prv, prv_sp = xn_ap(par, ct, -1, CH)
                    dd, dd_sp = A_bf(DD, 512)
                    op("vector", lambda e, o=dd, a=cur, b=prv: e.tensor_sub(out=o, in0=a, in1=b),
                       reads=cur_sp + prv_sp, writes=dd_sp)
                    op("vector", lambda e, o=dst[:, ct * 512: ct * 512 + CH], a=dd, b=prv,
                       sc=VPP[:, vcol_base + ct: vcol_base + ct + 1]:
                       e.scalar_tensor_tensor(out=o, in0=a, scalar=sc, in1=b,
                                              op0=ALU.mult, op1=ALU.add),
                       reads=dd_sp + prv_sp + [("vpp",) + _bs(vcol_base, vcol_base + 8, 4)],
                       writes=dst_sp)

            mixes(0, xr, xr_sp)
            rf, rf_sp = A_bf(RF, CT * 512)
            for dt in range(CT):
                bank = dt % 2
                for ct in range(CT):
                    lhs, lhs_sp = wsl(wr_s, ct, dt * 128, (dt + 1) * 128)
                    mm(PB[bank][:, 0:CH], lhs, xr[:, ct * 512: ct * 512 + CH],
                       reads=lhs_sp + xr_sp, writes=PBr(bank),
                       start=(ct == 0), stop=(ct == CT - 1))
                op("scalar", lambda e, o=rf[:, dt * 512: dt * 512 + CH], b=PB[bank][:, 0:CH]:
                   e.activation(out=o, in_=b, func=AF.Sigmoid),
                   reads=PBr(bank), writes=rf_sp)
            mixes(8, xr, xr_sp)
            kact, kact_sp = A_bf(KACT, CT * 512)
            for dt in range(CT):
                bank = dt % 2
                for ct in range(CT):
                    lhs, lhs_sp = wsl(wk_s, ct, dt * 128, (dt + 1) * 128)
                    mm(PB[bank][:, 0:CH], lhs, xr[:, ct * 512: ct * 512 + CH],
                       reads=lhs_sp + xr_sp, writes=PBr(bank),
                       start=(ct == 0), stop=(ct == CT - 1))
                kt, kt_sp = A_f32(KT, 512)
                op("vector", lambda e, o=kt, b=PB[bank][:, 0:CH]:
                   e.tensor_single_scalar(out=o, in_=b, scalar=0.0, op=ALU.max),
                   reads=PBr(bank), writes=kt_sp)
                op("vector", lambda e, o=kact[:, dt * 512: dt * 512 + CH], a=kt:
                   e.tensor_mul(out=o, in0=a, in1=a),
                   reads=kt_sp, writes=kact_sp)
            for dt in range(CT):
                bank = dt % 2
                for ct in range(CT):
                    lhs, lhs_sp = wsl(wv_s, ct, dt * 128, (dt + 1) * 128)
                    mm(PB[bank][:, 0:CH], lhs, kact[:, ct * 512: ct * 512 + CH],
                       reads=lhs_sp + kact_sp, writes=PBr(bank),
                       start=(ct == 0), stop=(ct == CT - 1))
                tb, tb_sp = A_f32(TMP + (dt % 2) * 512, 512)
                op("vector", lambda e, o=tb, b=PB[bank][:, 0:CH],
                   r_=rf[:, dt * 512: dt * 512 + CH]:
                   e.tensor_tensor(out=o, in0=b, in1=r_, op=ALU.mult),
                   reads=PBr(bank) + rf_sp, writes=tb_sp)
                ap = R[:, dt * T + t0: dt * T + t0 + CH]
                sp = [("R",) + _bs(dt * T + t0, dt * T + t0 + CH, 4)]
                op("vector", lambda e, o=ap, a=tb: e.tensor_add(out=o, in0=o, in1=a),
                   reads=sp + tb_sp, writes=sp)

    def attn_b(sl, midload=None):
        wr_s, wk_s, wv_s, wo_s = sl
        arena_reset()
        NUM = arena(2056)
        DEN = arena(2056)
        AA = arena(2048)
        EK = arena(2048)
        XM = arena(1024)
        RF = arena(1024)
        SLC = arena(1024)
        DD = arena(128)
        SQD = arena(256)
        num, num_sp = A_f32(NUM, CT * 257)
        den, den_sp = A_f32(DEN, CT * 257)
        for ct in range(CT):
            op("gpsimd", lambda e, o=num[:, ct * 257:ct * 257 + 1]: e.memset(o, 0.0),
               writes=[("arena",) + _bs(NUM + ct * 257, NUM + ct * 257 + 1, 4)])
            op("gpsimd", lambda e, o=den[:, ct * 257:ct * 257 + 1]: e.memset(o, 0.0),
               writes=[("arena",) + _bs(DEN + ct * 257, DEN + ct * 257 + 1, 4)])
        op("vector", lambda e: e.memset(GNS[:], 0.0), writes=[("gns", 0, 256)])
        op("vector", lambda e: e.memset(GNQ[:], 0.0), writes=[("gnq", 0, 256)])

        nH = T // CH

        def ln_b(H):
            par = H % 2
            ln_stats(H * CH, CH, par, LNF[par][0], LNF[par][1])
            ln_apply_full(H * CH, CH, par, "zero" if H == 0 else "carry",
                          LNF[par][1], LNF[par][2])

        ln_b(0)
        for H in range(nH):
            par = H % 2
            if H + 1 < nH:
                ln_b(H + 1)
            for sub in range(2):
                h = H * 2 + sub
                t0 = h * CHB
                xoff = sub * CHB

                dfull, dfull_sp = A_bf(SLC, CT * 256)
                for ct in range(CT):
                    cur, cur_sp = xn_ap(par, ct, xoff, CHB)
                    prv, prv_sp = xn_ap(par, ct, xoff - 1, CHB)
                    op("vector", lambda e, o=dfull[:, ct * 256:(ct + 1) * 256], a=cur, b=prv:
                       e.tensor_sub(out=o, in0=a, in1=b),
                       reads=cur_sp + prv_sp, writes=dfull_sp)

                def mixes(vcol_base, dst, dst_sp):
                    for ct in range(CT):
                        prv, prv_sp = xn_ap(par, ct, xoff - 1, CHB)
                        op("vector", lambda e, o=dst[:, ct * 256: ct * 256 + CHB],
                           a=dfull[:, ct * 256:(ct + 1) * 256], b=prv,
                           sc=VPP[:, vcol_base + ct: vcol_base + ct + 1]:
                           e.scalar_tensor_tensor(out=o, in0=a, scalar=sc, in1=b,
                                                  op0=ALU.mult, op1=ALU.add),
                           reads=dfull_sp + prv_sp + [("vpp",) + _bs(vcol_base, vcol_base + 8, 4)],
                           writes=dst_sp)

                xm, xm_sp = A_bf(XM, CT * 256)
                rf, rf_sp = A_bf(RF, CT * 256)
                ek, ek_sp = A_f32(EK, CT * 256)
                aa, aa_sp = A_f32(AA, CT * 256)
                mixes(16, xm, xm_sp)
                for dt in range(CT):
                    bank = dt % 2
                    for ct in range(CT):
                        lhs, lhs_sp = wsl(wr_s, ct, dt * 128, (dt + 1) * 128)
                        mm(PB[bank][:, 0:CHB], lhs, xm[:, ct * 256: ct * 256 + CHB],
                           reads=lhs_sp + xm_sp, writes=PBr(bank),
                           start=(ct == 0), stop=(ct == CT - 1))
                    op("scalar", lambda e, o=rf[:, dt * 256: dt * 256 + CHB], b=PB[bank][:, 0:CHB]:
                       e.activation(out=o, in_=b, func=AF.Sigmoid),
                       reads=PBr(bank), writes=rf_sp)
                mixes(24, xm, xm_sp)
                for dt in range(CT):
                    bank = dt % 2
                    for ct in range(CT):
                        lhs, lhs_sp = wsl(wk_s, ct, dt * 128, (dt + 1) * 128)
                        mm(PB[bank][:, 0:CHB], lhs, xm[:, ct * 256: ct * 256 + CHB],
                           reads=lhs_sp + xm_sp, writes=PBr(bank),
                           start=(ct == 0), stop=(ct == CT - 1))
                    op("scalar", lambda e, o=ek[:, dt * 256: dt * 256 + CHB], b=PB[bank][:, 0:CHB]:
                       e.activation(out=o, in_=b, func=AF.Exp),
                       reads=PBr(bank), writes=ek_sp)
                mixes(32, xm, xm_sp)
                for dt in range(CT):
                    bank = dt % 2
                    for ct in range(CT):
                        lhs, lhs_sp = wsl(wv_s, ct, dt * 128, (dt + 1) * 128)
                        mm(PB[bank][:, 0:CHB], lhs, xm[:, ct * 256: ct * 256 + CHB],
                           reads=lhs_sp + xm_sp, writes=PBr(bank),
                           start=(ct == 0), stop=(ct == CT - 1))
                    op("vector", lambda e, o=aa[:, dt * 256: dt * 256 + CHB], b=PB[bank][:, 0:CHB],
                       k_=ek[:, dt * 256: dt * 256 + CHB]:
                       e.tensor_tensor(out=o, in0=b, in1=k_, op=ALU.mult),
                       reads=PBr(bank) + ek_sp, writes=aa_sp)
                slc, slc_sp = A_bf(SLC, CT * 256)
                for ct in range(CT):
                    n0 = ct * 257
                    numc = [("arena",) + _bs(NUM + n0, NUM + n0 + 1, 4)]
                    denc = [("arena",) + _bs(DEN + n0, DEN + n0 + 1, 4)]
                    nums = [("arena",) + _bs(NUM + n0, NUM + n0 + 257, 4)]
                    dens = [("arena",) + _bs(DEN + n0, DEN + n0 + 257, 4)]
                    if h > 0:
                        op("gpsimd", lambda e, o=num[:, n0:n0 + 1], i_=num[:, n0 + 256:n0 + 257]:
                           e.tensor_copy(out=o, in_=i_), reads=nums, writes=numc)
                        op("gpsimd", lambda e, o=den[:, n0:n0 + 1], i_=den[:, n0 + 256:n0 + 257]:
                           e.tensor_copy(out=o, in_=i_), reads=dens, writes=denc)
                    wv_bc = VPPC[:, 8 + ct:9 + ct].broadcast_to((128, CHB))
                    op("vector", lambda e, o=num[:, n0 + 1:n0 + 257],
                       d0=wv_bc, d1=aa[:, ct * 256:(ct + 1) * 256],
                       ini=num[:, n0:n0 + 1]:
                       e.tensor_tensor_scan(out=o, data0=d0, data1=d1, initial=ini,
                                            op0=ALU.mult, op1=ALU.add),
                       reads=[("vppc", 32, 64)] + aa_sp + numc, writes=nums)
                    op("vector", lambda e, o=den[:, n0 + 1:n0 + 257],
                       d0=wv_bc, d1=ek[:, ct * 256:(ct + 1) * 256],
                       ini=den[:, n0:n0 + 1]:
                       e.tensor_tensor_scan(out=o, data0=d0, data1=d1, initial=ini,
                                            op0=ALU.mult, op1=ALU.add),
                       reads=[("vppc", 32, 64)] + ek_sp + denc, writes=dens)
                    op("vector", lambda e, o=aa[:, ct * 256:(ct + 1) * 256],
                       sh=num[:, n0:n0 + 256], sc=VPPC[:, ct:ct + 1]:
                       e.scalar_tensor_tensor(out=o, in0=o, scalar=sc, in1=sh,
                                              op0=ALU.mult, op1=ALU.add),
                       reads=aa_sp + [("vppc", 0, 32)] + nums, writes=aa_sp)
                    op("vector", lambda e, o=ek[:, ct * 256:(ct + 1) * 256],
                       sh=den[:, n0:n0 + 256], sc=VPPC[:, ct:ct + 1]:
                       e.scalar_tensor_tensor(out=o, in0=o, scalar=sc, in1=sh,
                                              op0=ALU.mult, op1=ALU.add),
                       reads=ek_sp + [("vppc", 0, 32)] + dens, writes=ek_sp)
                op("scalar", lambda e, o=ek[:, 0:CT * 256]:
                   e.activation(out=o, in_=o, func=AF.Identity, bias=EPSW[0:128]),
                   reads=ek_sp + [("epsw", 0, 4)], writes=ek_sp)
                op("vector", lambda e, o=ek[:, 0:CT * 256]: e.reciprocal(out=o, in_=o),
                   reads=ek_sp, writes=ek_sp)
                op("vector", lambda e, o=aa[:, 0:CT * 256], b=ek[:, 0:CT * 256]:
                   e.tensor_tensor(out=o, in0=o, in1=b, op=ALU.mult),
                   reads=aa_sp + ek_sp, writes=aa_sp)
                op("vector", lambda e, o=slc, a=aa[:, 0:CT * 256], r_=rf[:, 0:CT * 256]:
                   e.tensor_tensor(out=o, in0=a, in1=r_, op=ALU.mult),
                   reads=aa_sp + rf_sp, writes=slc_sp)
                op("vector", lambda e, o=GNS[:].rearrange("q (a b) -> q a b", b=8)[:, :, h:h + 1],
                   i_=slc.rearrange("q (a b) -> q a b", a=8):
                   e.tensor_reduce(out=o, in_=i_, axis=mybir.AxisListType.X, op=ALU.add),
                   reads=slc_sp, writes=[("gns", 0, 256)])
                for ct in range(CT):
                    gcol = ct * 8 + h
                    sqd, sqd_sp = A_f32(SQD, 256)
                    op("scalar", lambda e, o=sqd, i_=slc[:, ct * 256:(ct + 1) * 256],
                       acc=GNQ[:, gcol:gcol + 1]:
                       e.activation(out=o, in_=i_, func=AF.Square, accum_out=acc),
                       reads=slc_sp, writes=sqd_sp + [("gnq",) + _bs(gcol, gcol + 1, 4)])
                op("gpsimd", lambda e, i_=slc, t0_=t0:
                   e.dma_start(out=slab_hbm[:].rearrange("p (c n) -> p c n", c=8)[:, :, t0_:t0_ + CHB],
                               in_=i_.rearrange("p (c n) -> p c n", c=8)),
                   reads=slc_sp, writes=[("slabh",) + _bs(t0, t0 + CHB, 2)], dma=True)
        if midload:
            midload()
        gsum, gsum_sp = A_f32(SQD, 16)
        for ct in range(CT):
            op("vector", lambda e, o=gsum[:, ct:ct + 1], i_=GNS[:, ct * 8:(ct + 1) * 8]:
               e.tensor_reduce(out=o, in_=i_, axis=mybir.AxisListType.X, op=ALU.add),
               reads=[("gns",) + _bs(ct * 8, (ct + 1) * 8, 4)],
               writes=[("arena",) + _bs(SQD + ct, SQD + ct + 1, 4)])
            op("vector", lambda e, o=gsum[:, 8 + ct:9 + ct], i_=GNQ[:, ct * 8:(ct + 1) * 8]:
               e.tensor_reduce(out=o, in_=i_, axis=mybir.AxisListType.X, op=ALU.add),
               reads=[("gnq",) + _bs(ct * 8, (ct + 1) * 8, 4)],
               writes=[("arena",) + _bs(SQD + 8 + ct, SQD + 8 + ct + 1, 4)])
        gsb, gsb_sp = A_bf(SQD + 16, 16)
        op("gpsimd", lambda e, o=gsb, i_=gsum: e.tensor_copy(out=o, in_=i_),
           reads=[("arena",) + _bs(SQD, SQD + 16, 4)], writes=gsb_sp)
        mm(PB[0][:, 0:16], BLK[:], gsb,
           reads=[("blk", 0, 256)] + gsb_sp, writes=PBr(0), start=True, stop=True)
        gscale = 1.0 / (T * 64)
        dt_sp = [("dtmp", 0, 128)]
        op("vector", lambda e: e.tensor_single_scalar(out=DTMP[:, 0:16], in_=PB[0][:, 0:16],
                                                      scalar=gscale, op=ALU.mult),
           reads=PBr(0), writes=dt_sp)
        op("gpsimd", lambda e: e.tensor_tensor(out=DTMP[:, 16:24], in0=DTMP[:, 0:8],
                                               in1=DTMP[:, 0:8], op=ALU.mult),
           reads=dt_sp, writes=dt_sp)
        op("gpsimd", lambda e: e.tensor_sub(out=DTMP[:, 8:16], in0=DTMP[:, 8:16],
                                            in1=DTMP[:, 16:24]),
           reads=dt_sp, writes=dt_sp)
        op("scalar", lambda e: e.activation(out=DTMP[:, 8:16], in_=DTMP[:, 8:16],
                                            func=AF.Sqrt, bias=EPSC[:]),
           reads=dt_sp + [("epsc", 0, 4)], writes=dt_sp)
        op("vector", lambda e: e.reciprocal(out=DTMP[:, 8:16], in_=DTMP[:, 8:16]),
           reads=dt_sp, writes=dt_sp)
        op("gpsimd", lambda e: e.tensor_tensor(out=AB[:, 0:8], in0=DTMP[:, 8:16],
                                               in1=VPP[:, 56:64], op=ALU.mult),
           reads=dt_sp + [("vpp", 224, 256)], writes=[("ab", 0, 32)])
        op("gpsimd", lambda e: e.tensor_tensor(out=DTMP[:, 16:24], in0=DTMP[:, 0:8],
                                               in1=AB[:, 0:8], op=ALU.mult),
           reads=dt_sp + [("ab", 0, 32)], writes=dt_sp)
        op("gpsimd", lambda e: e.tensor_sub(out=AB[:, 8:16], in0=VPP[:, 64:72],
                                            in1=DTMP[:, 16:24]),
           reads=[("vpp", 256, 288)] + dt_sp, writes=[("ab", 32, 64)])
        for h in range(T // CHB):
            t0 = h * CHB
            slc, slc_sp = A_bf(SLC, CT * 256)
            yc, yc_sp = A_bf(AA, CT * 512)
            op("gpsimd", lambda e, o=slc, t0_=t0:
               e.dma_start(out=o.rearrange("p (c n) -> p c n", c=8),
                           in_=slab_hbm[:].rearrange("p (c n) -> p c n", c=8)[:, :, t0_:t0_ + CHB]),
               reads=[("slabh",) + _bs(t0, t0 + CHB, 2)], writes=slc_sp, dma=True)
            for ct in range(CT):
                op("vector", lambda e, o=yc[:, ct * 256:(ct + 1) * 256],
                   i_=slc[:, ct * 256:(ct + 1) * 256],
                   al=AB[:, ct:ct + 1], be=AB[:, 8 + ct:9 + ct]:
                   e.tensor_scalar(out=o, in0=i_, scalar1=al, scalar2=be,
                                   op0=ALU.mult, op1=ALU.add),
                   reads=slc_sp + [("ab", 0, 64)], writes=yc_sp)
            for dt in range(CT):
                bank = dt % 2
                for ct in range(CT):
                    lhs, lhs_sp = wsl(wo_s, ct, dt * 128, (dt + 1) * 128)
                    mm(PB[bank][:, 0:CHB], lhs, yc[:, ct * 256: ct * 256 + CHB],
                       reads=lhs_sp + yc_sp, writes=PBr(bank),
                       start=(ct == 0), stop=(ct == CT - 1))
                r_update(dt, t0, CHB, bank)
    WIDX = dict(A_wk=0, A_wv=1, A_wq=2, A_wg=3, A_wo=4,
                F_wr=5, F_wk=6, F_wv=7, B_wr=8, B_wk=9, B_wv=10, B_wo=11)
    for s in range(2):
        cur_seq[0] = s
        if s > 0:
            for ct in range(CT):
                op("gpsimd", lambda e, s_=s, ct_=ct:
                   e.dma_start(out=R[:, ct_ * T:(ct_ + 1) * T],
                               in_=x_in[s_, ct_ * 128:(ct_ + 1) * 128, :]),
                   writes=[("R",) + _bs(ct * T, (ct + 1) * T, 4)], dma=True)
        def loadw(names_slots):
            for name, slot in names_slots:
                load_weight(WIDX[name], slot)
        loadw([("A_wk", 0), ("A_wv", 1), ("A_wq", 2)])
        if "attn_a" in parts:
            attn_a((0, 1, 2, 0, 1),
                   midload=lambda: loadw([("A_wg", 0), ("A_wo", 1)]))
        else:
            loadw([("A_wg", 0), ("A_wo", 1)])
        loadw([("F_wr", 2), ("F_wk", 0), ("F_wv", 1)])
        if "ffn1" in parts:
            ffn((2, 0, 1))
        loadw([("B_wr", 2), ("B_wk", 0), ("B_wv", 1)])
        if "attn_b" in parts:
            attn_b((2, 0, 1, 2), midload=lambda: loadw([("B_wo", 2)]))
        else:
            loadw([("B_wo", 2)])
        loadw([("F_wr", 0), ("F_wk", 1), ("F_wv", 2)])
        if "ffn2" in parts:
            ffn((0, 1, 2))
        for ct in range(CT):
            op("gpsimd", lambda e, s_=s, ct_=ct:
               e.dma_start(out=y_out[s_, ct_ * 128:(ct_ + 1) * 128, :],
                           in_=R[:, ct_ * T:(ct_ + 1) * T]),
               reads=[("R",) + _bs(ct * T, (ct + 1) * T, 4)],
               writes=[("hbm_y",) + _bs(s * CT * T + ct * T, s * CT * T + (ct + 1) * T, 4)],
               dma=True)
    op("gpsimd", lambda e: e.nop(), reads=[("hbm_y", 0, 2 * CT * T * 4)])

    emit = P.finalize(sems)
    with nc.Block() as block:
        @block.sync
        def _(sync):
            for f in emit["sync"]:
                f(sync)

        @block.vector
        def _(vector):
            for f in emit["vector"]:
                f(vector)

        @block.scalar
        def _(scalar):
            for f in emit["scalar"]:
                f(scalar)

        @block.tensor
        def _(tensor):
            for f in emit["tensor"]:
                f(tensor)

        @block.gpsimd
        def _(gpsimd):
            for f in emit["gpsimd"]:
                f(gpsimd)

    return nc, stack


_CACHE = {}


def kernel(L0_attn_a_x, L0_attn_a_wg_w, L0_attn_a_wg_b, L0_attn_a_wq_w,
           L0_attn_a_wq_b, L0_attn_a_wk_w, L0_attn_a_wk_b, L0_attn_a_wv_w,
           L0_attn_a_wv_b, L0_attn_a_out_w, L0_attn_a_out_b,
           L0_attn_b_time_mix_r, L0_attn_b_time_mix_k, L0_attn_b_time_mix_v,
           L0_attn_b_time_decay, L0_attn_b_time_first, L0_attn_b_W_r,
           L0_attn_b_W_k, L0_attn_b_W_v, L0_attn_b_W_o, L0_attn_b_gn_weight,
           L0_attn_b_gn_bias, L0_ffn_W_r, L0_ffn_W_k, L0_ffn_W_v, L0_ffn_mu_r,
           L0_ffn_mu_k):
    x = np.asarray(L0_attn_a_x, dtype=np.float32)
    assert x.shape == (16, 2048, 1024)

    f32 = lambda a: np.ascontiguousarray(np.asarray(a), dtype=np.float32)
    wall = np.ascontiguousarray(np.stack([
        f32(L0_attn_a_wk_w), f32(L0_attn_a_wv_w), f32(L0_attn_a_wq_w),
        f32(L0_attn_a_wg_w), f32(L0_attn_a_out_w),
        f32(L0_ffn_W_r).T, f32(L0_ffn_W_k).T, f32(L0_ffn_W_v).T,
        f32(L0_attn_b_W_r).T, f32(L0_attn_b_W_k).T, f32(L0_attn_b_W_v).T,
        f32(L0_attn_b_W_o).T,
    ]).astype(np.float32))

    def pp(v):
        return f32(v).reshape(8, 128).T

    vpp = np.ascontiguousarray(np.concatenate([
        pp(L0_ffn_mu_r), pp(L0_ffn_mu_k),
        pp(L0_attn_b_time_mix_r), pp(L0_attn_b_time_mix_k), pp(L0_attn_b_time_mix_v),
        pp(L0_attn_b_time_decay), pp(L0_attn_b_time_first),
        pp(L0_attn_b_gn_weight), pp(L0_attn_b_gn_bias),
    ], axis=1), dtype=np.float32)
    vrow = np.concatenate([
        f32(L0_attn_a_wq_b).reshape(-1), f32(L0_attn_a_wk_b).reshape(-1),
        f32(L0_attn_a_wv_b).reshape(-1), f32(L0_attn_a_wg_b).reshape(-1),
        f32(L0_attn_a_out_b).reshape(-1),
    ]).reshape(1, 5120).astype(np.float32)
    ident = np.eye(128, dtype=np.float32)

    key = tuple(PARTS)
    if key not in _CACHE:
        _CACHE[key] = build_program(PARTS)
    nc, _stack = _CACHE[key]

    in_maps = []
    for cid in range(NCORES):
        xc = np.ascontiguousarray(x[2 * cid:2 * cid + 2].transpose(0, 2, 1))
        in_maps.append(dict(x=xc, wall=wall, vpp=vpp, vrow=vrow, ident=ident))

    res = run_bass_kernel_spmd(nc, in_maps, core_ids=list(range(NCORES)))
    out = np.empty((16, 2048, 1024), dtype=np.float32)
    for cid in range(NCORES):
        yc = np.asarray(res.results[cid]["y"])
        out[2 * cid:2 * cid + 2] = yc.transpose(0, 2, 1)
    return out
